# version 02
# baseline (speedup 1.0000x reference)
import numpy as np
import ml_dtypes
import concourse.bass as bass
import concourse.mybir as mybir
from concourse.bass_utils import run_bass_kernel_spmd
from concourse.aot_env import allow_ici_all_to_all

F32 = mybir.dt.float32
BF16 = mybir.dt.bfloat16
AF = mybir.ActivationFunctionType
OP = mybir.AluOpType

BF = ml_dtypes.bfloat16

ENGINES = ("tensor", "vector", "scalar", "gpsimd", "sync")

B_, S_, D_ = 2, 8192, 2048
FF = 8192
NH, HD = 16, 128
BIG_SP = 50.0 * np.sqrt(8.0)
C_SP = 10.0
EPS_LN = 1e-5
EPS_RMS = 1e-6
EPS_ATT = 1e-5


class Plan:

    def __init__(self, nc):
        self.nc = nc
        self.q = {e: [] for e in ENGINES}
        self.sem_h = {}
        self.sem_n = {}

    def sem(self, name):
        self.sem_h[name] = self.nc.alloc_semaphore(name=name)
        self.sem_n[name] = 0
        return name

    def wait(self, eng, sname, tgt=None):
        if tgt is None:
            tgt = self.sem_n[sname]
        h = self.sem_h[sname]
        self.q[eng].append(lambda e, h=h, t=tgt: e.wait_ge(h, t))
        return tgt

    def op(self, eng, fn, sname=None, k=1):
        if sname is None:
            self.q[eng].append(fn)
            return None
        h = self.sem_h[sname]
        self.sem_n[sname] += k

        def wrapped(e, fn=fn, h=h, k=k):
            ins = fn(e)
            ins.then_inc(h, k)
        self.q[eng].append(wrapped)
        return self.sem_n[sname]

    def selfsem(self, eng):
        sself = "self_" + eng
        if sself not in self.sem_h:
            self.sem_h[sself] = self.nc.alloc_semaphore(name=sself)
            self.sem_n[sself] = 0
        return sself

    def sop(self, eng, fn):
        sself = self.selfsem(eng)
        self.wait(eng, sself)
        return self.op(eng, fn, sself)

    def dma(self, eng, out, in_, sname, **kw):
        return self.op(eng, lambda e: e.dma_start(out=out, in_=in_, **kw),
                       sname, 16)

    def val(self, sname):
        return self.sem_n[sname]


def bcast_row(P, row_ap, out_ps, out_sb, ones_row, n, psname, evname,
              evac_eng="vector"):
    assert n <= 512
    P.op("tensor", lambda e: e.matmul(out_ps[:, 0:n], ones_row[:, :],
                                      row_ap, start=True, stop=True), psname)
    P.wait(evac_eng, psname)
    P.op(evac_eng, lambda e: e.tensor_copy(out_sb[:, 0:n], out_ps[:, 0:n]),
         evname)


def build(stage=10, dbg=()):
    nc = bass.Bass("TRN2", target_bir_lowering=False)
    P = Plan(nc)
    dbg = set(dbg)

    def di(name, shape, dt=F32):
        return nc.dram_tensor(name, list(shape), dt, kind="ExternalInput").ap()

    xq = di("xq", (2048, 2048))
    xm = di("xm", (2048, 2048))
    wq_t = di("wq_t", (128, 8192))
    wk_t = di("wk_t", (128, 8192))
    wv_t = di("wv_t", (128, 8192))
    wo_s = di("wo_s", (128, 8192))
    a_t = di("a_t", (128, 256))
    wi_t = di("wi_t", (128, 131072))
    wu_t = di("wu_t", (128, 131072))
    wd_t = di("wd_t", (128, 131072))
    bmat = di("bmat", (16, 8192))
    ln1r = di("ln1r", (1, 2048))
    ln2r = di("ln2r", (1, 2048))
    qkvw = di("qkvw", (16, 48))
    qk6 = di("qk6", (8, 96))
    pw1 = di("pw1", (8, 16))
    pw2 = di("pw2", (8, 16))
    cb0t = di("cb0t", (8, 1))
    cb1t = di("cb1t", (8, 1))
    cw0t = di("cw0t", (8, 3))
    cw1t = di("cw1t", (8, 3))
    cb0 = di("cb0", (1, 8))
    cb1 = di("cb1", (1, 8))
    pwm = di("pwm", (16, 16))
    pbr = di("pbr", (1, 16))
    lw16 = di("lw16", (1, 16))
    lbk = di("lbk", (1, 2))
    identw = di("identw", (128, 128), BF16)
    cmask = di("cmask", (512, 512))
    ind1b = di("ind1b", (16, 512), BF16)
    ind2b = di("ind2b", (16, 512), BF16)
    bind1 = di("bind1", (16, 128), BF16)
    bind2 = di("bind2", (16, 128), BF16)
    sl1 = di("sl1", (128, 128), BF16)
    sr1 = di("sr1", (128, 128), BF16)
    sl2 = di("sl2", (128, 128), BF16)
    sr2 = di("sr2", (128, 128), BF16)

    y = nc.dram_tensor("y", [2048, 2048], F32, kind="ExternalOutput").ap()

    dbg_out = {}
    def mkdbg(name, shape, dt=F32):
        if name in dbg and name not in dbg_out:
            dbg_out[name] = nc.dram_tensor("dbg_" + name, list(shape), dt,
                                           kind="ExternalOutput").ap()
        return dbg_out.get(name)

    def ds(name, shape, dt=BF16):
        return nc.dram_tensor(name, list(shape), dt).ap()

    xn_scr = ds("xn_scr", (2048, 2048))
    x1_scr = ds("x1_scr", (2048, 2048))
    ag_in = ds("ag_in", (2048, 2048))
    ag_out = ds("ag_out", (8192, 2048))
    mp_in = ds("mp_in", (544, 1024), F32)
    mp_out = ds("mp_out", (544, 1024), F32)
    s_in = ds("s_in", (2048, 2048))
    s_out = ds("s_out", (2048, 2048))
    h_scr = ds("h_scr", (2048, 2048))
    hT_scr = ds("hT_scr", (2048, 2048))
    xnm_scr = ds("xnm_scr", (2048, 2048))
    mlp_scr = ds("mlp_scr", (2048, 1024))
    mp_stage = ds("mp_stage", (68, 8192), F32)
    e_scr = ds("e_scr", (16384, 128))
    den_scr = ds("den_scr", (512, 128))

    sb = {}
    ctx_stack = []

    def alloc(name, shape, dt):
        t = nc.sbuf_tensor(name, list(shape), dt)
        h = t.__enter__()
        ctx_stack.append(t)
        sb[name] = h
        return h

    ident = alloc("ident", (128, 128), BF16)
    identf = alloc("identf", (128, 128), F32)
    ones_col = alloc("ones_col", (128, 1), BF16)
    ones_row = alloc("ones_row", (1, 128), BF16)
    ones512 = alloc("ones512", (1, 512), BF16)
    lnb = alloc("lnb", (128, 2048), BF16)
    cmask_sb = alloc("cmask_sb", (128, 2048), BF16)
    BB = [alloc(f"BB{i}", (128, 8192), BF16) for i in range(4)]
    FA = [alloc(f"FA{i}", (128, 8192), F32) for i in range(2)]
    xt = [alloc(f"xt{i}", (128, 2064), F32) for i in range(2)]
    wtile = [alloc(f"wt{i}", (128, 2048), BF16) for i in range(4)]
    small = alloc("small", (128, 64), F32)
    small2 = alloc("small2", (128, 64), F32)
    tb = [alloc(f"tb{i}", (128, 512), BF16) for i in range(4)]
    tf = [alloc(f"tf{i}", (128, 512), F32) for i in range(2)]
    rowf = alloc("rowf", (4, 512), F32)
    sprow = alloc("sprow", (64, 64), F32)
    spc = alloc("spc", (64, 512), BF16)
    modpre_sb = alloc("modpre_sb", (17, 512), F32)
    a_sb = alloc("a_sb", (128, 256), BF16)
    woa_sb = alloc("woa_sb", (128, 64), BF16)
    bmat_sb = alloc("bmat_sb", (16, 2048), BF16)
    bind1_sb = alloc("bind1_sb", (16, 128), BF16)
    bind2_sb = alloc("bind2_sb", (16, 128), BF16)
    ind1_sb = alloc("ind1_sb", (16, 512), BF16)
    ind2_sb = alloc("ind2_sb", (16, 512), BF16)
    qkt_sb = alloc("qkt_sb", (8, 96), BF16)
    spw = alloc("spw", (128, 512), BF16)
    spbias = alloc("spbias", (128, 1), F32)
    epsr = alloc("epsr", (128, 1), F32)
    lwb = alloc("lwb", (128, 16), F32)
    modac = alloc("modac", (128, 260), F32)
    m2t = alloc("m2t", (16, 2048), BF16)
    shift_sb = alloc("shift_sb", (128, 768), BF16)

    ps = {}
    def palloc(name, shape, dt=F32):
        t = nc.psum_tensor(name, list(shape), dt)
        h = t.__enter__()
        ctx_stack.append(t)
        ps[name] = h
        return h

    pA = palloc("pA", (128, 512))
    pB = palloc("pB", (128, 512))
    pC = palloc("pC", (128, 512))
    pD = palloc("pD", (128, 512))
    pE = palloc("pE", (128, 512))
    pF = palloc("pF", (128, 512))
    pT = palloc("pT", (128, 512), BF16)
    pS = palloc("pS", (128, 512))

    for s in ["sLD", "sST", "sWL", "sMM", "sVE", "sSC", "sGP", "sCC",
              "sAux", "sDB"]:
        P.sem(s)

    P.dma("sync", ident[:, :], identw[:, :], "sLD")
    P.dma("gpsimd", identf[:, :], identw[:, :], "sGP")
    P.dma("gpsimd", cmask_sb[:, :].rearrange("p (m n) -> p m n", m=4),
          cmask.rearrange("(m p) n -> p m n", p=128), "sGP")
    P.op("vector", lambda e: e.memset(ones_col[:, :], 1.0), "sVE")
    P.op("vector", lambda e: e.memset(ones_row[:, :], 1.0), "sVE")
    P.op("vector", lambda e: e.memset(ones512[:, :], 1.0), "sVE")
    P.op("vector", lambda e: e.memset(xt[0][:, 2048:2064], 0.0), "sVE")
    P.op("vector", lambda e: e.memset(xt[1][:, 2048:2064], 0.0), "sVE")

    ln1row = alloc("ln1row", (1, 2048), BF16)
    ln2row = alloc("ln2row", (1, 2048), BF16)
    P.dma("gpsimd", ln1row[:, :], ln1r[:, :], "sGP")
    P.dma("gpsimd", ln2row[:, :], ln2r[:, :], "sGP")
    P.wait("tensor", "sGP")
    P.wait("tensor", "sVE")
    for i in range(4):
        P.op("tensor", lambda e, i=i: e.matmul(
            pA[:, 0:512], ones_row[:, :], ln1row[:, 512 * i:512 * (i + 1)],
            start=True, stop=True), "sMM")
        P.wait("vector", "sMM")
        P.op("vector", lambda e, i=i: e.tensor_copy(
            lnb[:, 512 * i:512 * (i + 1)], pA[:, 0:512]), "sVE")
        P.wait("tensor", "sVE")

    ldw0 = P.val("sLD")

    P.wait("sync", "sLD", ldw0)

    def p0_tile(i, src, base_mul_sb, dst_a, dst_b):
        t = xt[i % 2]
        o = 32 * (i % 2)

        def sm(j):
            return small[:, o + j:o + j + 1]
        if i >= 2:
            P.wait("sync", P.selfsem("vector"))
            P.wait("sync", P.selfsem("scalar"))
        P.dma("sync", t[:, 0:2048], src[128 * i:128 * (i + 1), :], "sLD")
        tgt = P.val("sLD")
        P.wait("scalar", "sLD", tgt)
        P.sop("scalar", lambda e: e.activation(
            FA[1][:, 0:2048], t[:, 0:2048], AF.Square))
        P.wait("vector", "sLD", tgt)
        P.sop("vector", lambda e: e.reduce_sum(
            sm(0), t[:, 0:2048], axis=mybir.AxisListType.X))
        P.wait("vector", P.selfsem("scalar"))
        P.sop("vector", lambda e: e.reduce_sum(
            sm(1), FA[1][:, 0:2048], axis=mybir.AxisListType.X))
        P.sop("vector", lambda e: e.tensor_scalar(
            sm(2), sm(0), 1.0 / 2048, None, OP.mult))
        P.sop("vector", lambda e: e.tensor_scalar(
            sm(3), sm(1), 1.0 / 2048, None, OP.mult))
        P.sop("vector", lambda e: e.tensor_tensor(
            sm(4), sm(2), sm(2), OP.mult))
        P.sop("vector", lambda e: e.tensor_tensor(
            sm(5), sm(3), sm(4), OP.subtract))
        P.sop("vector", lambda e: e.tensor_scalar(
            sm(6), sm(5), EPS_LN, None, OP.add))
        P.wait("scalar", P.selfsem("vector"))
        P.sop("scalar", lambda e: e.activation(sm(7), sm(6), AF.Ln))
        P.sop("scalar", lambda e: e.activation(sm(12), sm(7), AF.Exp,
                                               scale=-0.5))
        P.sop("scalar", lambda e: e.activation(sm(8), sm(7), AF.Exp,
                                               scale=-1.0))
        P.wait("vector", P.selfsem("scalar"))
        P.sop("vector", lambda e: e.tensor_scalar(
            sm(9), sm(5), sm(8), EPS_RMS, OP.mult, OP.add))
        P.wait("scalar", P.selfsem("vector"))
        P.sop("scalar", lambda e: e.activation(sm(10), sm(9), AF.Ln))
        P.sop("scalar", lambda e: e.activation(sm(13), sm(10), AF.Exp,
                                               scale=-0.5))
        P.wait("vector", P.selfsem("scalar"))
        P.sop("vector", lambda e: e.tensor_tensor(
            sm(14), sm(12), sm(13), OP.mult))
        P.sop("vector", lambda e: e.tensor_scalar(
            sm(15), sm(2), sm(12), -1.0, OP.mult, OP.mult))
        P.sop("vector", lambda e: e.tensor_scalar(
            sm(16), sm(2), sm(14), -1.0, OP.mult, OP.mult))
        P.wait("scalar", P.selfsem("vector"))
        P.wait("scalar", "sST")
        P.sop("scalar", lambda e: e.activation(
            BB[0][:, 0:2048], t[:, 0:2048], AF.Identity,
            bias=sm(15), scale=sm(12)))
        P.wait("vector", "sST")
        P.sop("vector", lambda e: e.tensor_scalar(
            FA[0][:, 0:2048], t[:, 0:2048], sm(14), sm(16),
            OP.mult, OP.add))
        P.sop("vector", lambda e: e.tensor_tensor(
            BB[1][:, 0:2048], FA[0][:, 0:2048], base_mul_sb[:, 0:2048],
            OP.mult))
        P.wait("sync", P.selfsem("scalar"))
        P.dma("sync", dst_a[128 * i:128 * (i + 1), :], BB[0][:, 0:2048], "sST")
        P.wait("sync", P.selfsem("vector"))
        P.dma("sync", dst_b[128 * i:128 * (i + 1), :], BB[1][:, 0:2048], "sST")

    for i in range(16):
        p0_tile(i, xq, lnb, xn_scr, x1_scr)

    P.wait("sync", "sST")
    for dsi in range(16):
        buf = wtile[dsi % 2]
        if dsi >= 2:
            P.wait("sync", "sST")
        P.dma("sync", buf[:, 0:2048],
              x1_scr[0:2048, 128 * dsi:128 * (dsi + 1)], "sLD",
              transpose=True)
        P.wait("sync", "sLD")
        P.dma("sync", ag_in[128 * dsi:128 * (dsi + 1), :], buf[:, 0:2048],
              "sST")

    def dump_scr(name, src, nrows, ncols, dt=BF16):
        mkdbg(name, (nrows, ncols), dt)
        if name not in dbg:
            return
        buf = wtile[3] if dt == BF16 else FA[0]
        for i in range(0, nrows, 128):
            r = min(128, nrows - i)
            P.wait("sync", "sDB")
            P.dma("sync", buf[0:r, 0:ncols], src[i:i + r, 0:ncols], "sDB")
            P.wait("sync", "sDB")
            P.dma("sync", dbg_out[name][i:i + r, :], buf[0:r, 0:ncols], "sDB")

    if "small" in dbg:
        mkdbg("small", (128, 64), F32)
        P.wait("sync", "sVE")
        P.wait("sync", "sSC")
        P.dma("sync", dbg_out["small"][:, :], small[:, :], "sDB")
    if "xn" in dbg or "x1" in dbg:
        P.wait("sync", "sST")
        dump_scr("xn", xn_scr, 2048, 2048)
        dump_scr("x1", x1_scr, 2048, 2048)



    P.wait("gpsimd", "sST")
    P.wait("gpsimd", "sDB")
    P.op("gpsimd", lambda e: e.collective_compute(
        "AllGather", OP.bypass,
        replica_groups=[[0, 1, 2, 3], [4, 5, 6, 7]],
        ins=[ag_in[:, :]], outs=[ag_out[:, :]]), "sCC")
    agdone = P.val("sCC")
    if "agout" in dbg:
        P.wait("sync", "sCC", agdone)
        dump_scr("agout", ag_out, 8192, 2048)

    if stage >= 2:
        build_p1(nc, P, locals())
        if "mpout" in dbg:
            P.wait("sync", "sCC")
            dump_scr("mpout", mp_out, 544, 1024, F32)
        if "mpstage" in dbg:
            P.wait("sync", "sST")
            dump_scr("mpstage", mp_stage, 68, 8192, F32)
    if stage >= 4:
        build_p2(nc, P, locals())
    if stage >= 3:
        build_p3(nc, P, locals())
    if stage >= 4 and "sout" in dbg:
        P.wait("sync", "sCC")
        dump_scr("sout", s_out, 2048, 2048)

    if stage < 3:
        P.dma("sync", y[0:128, :], xt[0][:, 0:2048], "sST")
    P.wait("sync", "sST")
    P.wait("sync", "sDB")
    P.wait("sync", "sGP")
    P.wait("sync", "sCC")

    with nc.Block() as block:
        for eng in ENGINES:
            dec = getattr(block, eng)
            def make(eng):
                def body(e):
                    for fn in P.q[eng]:
                        fn(e)
                return body
            dec(make(eng))

    for t in reversed(ctx_stack):
        t.__exit__(None, None, None)
    return nc, dbg_out


def build_p1(nc, P, env):
    g = env
    BB, FA, tb, tf = g["BB"], g["FA"], g["tb"], g["tf"]
    wtile, small, small2, rowf = g["wtile"], g["small"], g["small2"], g["rowf"]
    ps = g["ps"]
    pA, pB, pC, pD, pE, pF = (ps[k] for k in
                              ["pA", "pB", "pC", "pD", "pE", "pF"])
    ones_col, ones_row, cmask_sb = g["ones_col"], g["ones_row"], g["cmask_sb"]
    a_sb, woa_sb = g["a_sb"], g["woa_sb"]
    ag_out = g["ag_out"]
    wq_t, wk_t, wv_t, wo_s, a_t = (g[k] for k in
                                   ["wq_t", "wk_t", "wv_t", "wo_s", "a_t"])
    mp_stage, mp_in, mp_out = g["mp_stage"], g["mp_in"], g["mp_out"]
    agdone = g["agdone"]
    AX = mybir.AxisListType.X
    SV, SS = P.selfsem("vector"), P.selfsem("scalar")

    P.dma("gpsimd", BB[3][:, :], wo_s[:, :], "sGP")
    P.dma("gpsimd", a_sb[:, :], a_t[:, :], "sGP")
    P.wait("tensor", "sGP")
    for dd in range(4):
        for ds2 in range(16):
            P.op("tensor", lambda e, dd=dd, ds2=ds2: e.matmul(
                pF[:, 0:16],
                BB[3][:, (ds2 * 4 + dd) * 128:(ds2 * 4 + dd) * 128 + 128],
                a_sb[:, ds2 * 16:ds2 * 16 + 16],
                start=(ds2 == 0), stop=(ds2 == 15)), "sMM")
        P.wait("vector", "sMM")
        P.op("vector", lambda e, dd=dd: e.tensor_copy(
            woa_sb[:, 16 * dd:16 * dd + 16], pF[:, 0:16]), "sVE")
        P.wait("tensor", "sVE")

    P.wait("sync", "sCC", agdone)

    for h in range(4):
        P.wait("gpsimd", "sMM")
        for wi, wsrc in ((0, wq_t), (1, wk_t), (2, wv_t)):
            P.dma("gpsimd",
                  wtile[wi][:, :].rearrange("p (d j) -> p d j", d=16),
                  wsrc.rearrange("p (d g j) -> p d g j", d=16, g=4)[:, :, h, :],
                  "sGP")
        wld = P.val("sGP")
        P.wait("tensor", "sGP", wld)

        P.wait("sync", "sMM")
        P.wait("tensor", "sVE")
        P.wait("tensor", "sSC")
        mm_t = {}
        ld_t = {}
        for tg in range(16):
            pk, pq, pv = (pA, pB, pC) if tg % 2 == 0 else (pD, pE, pF)
            if tg >= 2:
                P.wait("tensor", "sVE")
                P.wait("tensor", "sSC")
            for ds in range(16):
                it = tg * 16 + ds
                xb = tb[it % 2]
                if it >= 2:
                    P.wait("sync", "sMM", mm_t[it - 2])
                q0 = 2048 * (tg // 4)
                c0 = 512 * (tg % 4)
                P.dma("sync", xb[:, :],
                      ag_out[q0 + 128 * ds:q0 + 128 * ds + 128, c0:c0 + 512],
                      "sLD")
                ld_t[it] = P.val("sLD")
                P.wait("tensor", "sLD", ld_t[it])
                P.op("tensor", lambda e, ds=ds, xb=xb, pk=pk: e.matmul(
                    pk[:, :], wtile[1][:, 128 * ds:128 * ds + 128], xb[:, :],
                    start=(ds == 0), stop=(ds == 15)), None)
                P.op("tensor", lambda e, ds=ds, xb=xb, pq=pq: e.matmul(
                    pq[:, :], wtile[0][:, 128 * ds:128 * ds + 128], xb[:, :],
                    start=(ds == 0), stop=(ds == 15)), None)
                for v in range(4):
                    P.op("tensor", lambda e, ds=ds, xb=xb, pv=pv, v=v: e.matmul(
                        pv[:, 128 * v:128 * v + 128],
                        xb[:, 128 * v:128 * v + 128],
                        wtile[2][:, 128 * ds:128 * ds + 128],
                        start=(ds == 0), stop=(ds == 15)),
                        "sMM" if v == 3 else None)
                mm_t[it] = P.val("sMM")
            P.wait("vector", "sMM", mm_t[tg * 16 + 15])
            P.op("vector", lambda e, tg=tg, pk=pk: e.tensor_copy(
                BB[0][:, 512 * tg:512 * tg + 512], pk[:, :]), "sVE")
            P.op("vector", lambda e, tg=tg, pv=pv: e.tensor_copy(
                BB[2][:, 512 * tg:512 * tg + 512], pv[:, :]), "sVE")
            P.wait("scalar", "sMM", mm_t[tg * 16 + 15])
            P.op("scalar", lambda e, tg=tg, pq=pq: e.activation(
                BB[1][:, 512 * tg:512 * tg + 512], pq[:, :], AF.Copy), "sSC")
        proj_ve, proj_sc = P.val("sVE"), P.val("sSC")

        P.wait("scalar", "sVE", proj_ve)
        for which, src in ((0, BB[0]), (1, BB[1])):
            P.sop("scalar", lambda e, src=src: e.activation(
                BB[3][:, 0:8192], src[:, 0:8192], AF.Square))
            P.wait("tensor", P.selfsem("scalar"))
            for ch in range(16):
                P.op("tensor", lambda e, ch=ch: e.matmul(
                    pF[0:1, 0:512], ones_col[:, :],
                    BB[3][:, 512 * ch:512 * ch + 512],
                    start=True, stop=True), "sMM")
                P.wait("vector", "sMM")
                P.sop("vector", lambda e, ch=ch, which=which: e.reduce_max(
                    small2[0:1, 40 + which:41 + which] if ch == 0 else
                    small2[0:1, 42:43], pF[0:1, 0:512], axis=AX))
                if ch > 0:
                    P.sop("vector", lambda e, which=which: e.tensor_tensor(
                        small2[0:1, 40 + which:41 + which],
                        small2[0:1, 40 + which:41 + which],
                        small2[0:1, 42:43], OP.max))
                P.wait("tensor", P.selfsem("vector"))
        P.wait("scalar", P.selfsem("vector"))
        P.sop("scalar", lambda e: e.activation(
            small2[0:1, 43:44], small2[0:1, 40:41], AF.Ln))
        P.sop("scalar", lambda e: e.activation(
            small2[0:1, 44:45], small2[0:1, 41:42], AF.Ln))
        P.wait("vector", P.selfsem("scalar"))
        P.sop("vector", lambda e: e.tensor_tensor(
            small2[0:1, 45:46], small2[0:1, 43:44], small2[0:1, 44:45],
            OP.add))
        P.sop("vector", lambda e: e.tensor_scalar(
            small2[0:1, 45:46], small2[0:1, 45:46], 0.5,
            float(-0.5 * np.log(128.0)), OP.mult, OP.add))
        P.wait("scalar", P.selfsem("vector"))
        P.sop("scalar", lambda e: e.activation(
            small2[0:1, 46:47], small2[0:1, 45:46], AF.Exp))
        P.wait("vector", P.selfsem("scalar"))
        P.sop("vector", lambda e: e.tensor_scalar(
            small2[0:1, 47:48], small2[0:1, 46:47], -1.0, None, OP.mult))
        P.sop("vector", lambda e: e.tensor_copy(
            tb[2][0:1, 0:1], small2[0:1, 47:48]))
        P.wait("tensor", P.selfsem("vector"))
        P.op("tensor", lambda e: e.matmul(
            pF[:, 0:1], ones_row[:, :], tb[2][0:1, 0:1],
            start=True, stop=True), "sMM")
        P.wait("vector", "sMM")
        P.wait("vector", "sSC")
        P.sop("vector", lambda e: e.tensor_copy(small2[:, 48:49], pF[:, 0:1]))
        bb_ve = P.val(P.selfsem("vector"))

        exp_t = {}
        pv_t = {}
        ev_ve = {}
        ev_sc = {}
        P.wait("scalar", P.selfsem("vector"), bb_ve)
        P.wait("tensor", "sVE")
        P.wait("tensor", "sSC")
        for qt in range(16):
            nk = 4 * qt + 4
            for sk in range(nk):
                psc = pA if sk % 2 == 0 else pB
                slot = tb[sk % 3]
                if sk >= 2:
                    P.wait("tensor", "sSC", exp_t[(qt, sk - 2)])
                elif qt > 0:
                    P.wait("tensor", "sSC", exp_t[(qt - 1, 4 * qt - 1)])
                P.op("tensor", lambda e, sk=sk, qt=qt, psc=psc: e.matmul(
                    psc[:, :], BB[0][:, 128 * sk:128 * sk + 128],
                    BB[1][:, 512 * qt:512 * qt + 512],
                    start=True, stop=True), "sMM")
                sc_t = P.val("sMM")
                m = sk - 4 * qt
                if m >= 0:
                    P.wait("vector", "sMM", sc_t)
                    P.op("vector", lambda e, psc=psc, m=m: e.tensor_tensor(
                        psc[:, :], psc[:, :], cmask_sb[:, 512 * m:512 * m + 512],
                        OP.add), "sVE")
                    P.wait("scalar", "sVE")
                else:
                    P.wait("scalar", "sMM", sc_t)
                if sk >= 3:
                    P.wait("scalar", "sMM", pv_t[(qt, sk - 3)])
                elif qt > 0:
                    P.wait("scalar", "sMM", pv_t[(qt - 1, 4 * qt - 1)])
                P.op("scalar", lambda e, psc=psc, slot=slot: e.activation(
                    slot[:, :], psc[:, :], AF.Exp,
                    bias=small2[:, 48:49], scale=float(1.0 / np.sqrt(128.0))),
                    "sSC")
                exp_t[(qt, sk)] = P.val("sSC")
                P.wait("tensor", "sSC", exp_t[(qt, sk)])
                if sk == 0 and qt > 0:
                    P.wait("tensor", "sSC", ev_sc[qt - 1])
                    P.wait("tensor", "sVE", ev_ve[qt - 1])
                P.op("tensor", lambda e, slot=slot, sk=sk, nk=nk: e.matmul(
                    pD[0:1, 0:512], ones_col[:, :], slot[:, :],
                    start=(sk == 0), stop=(sk == nk - 1)), None)
                P.op("tensor", lambda e, slot=slot, sk=sk, nk=nk: e.matmul(
                    pC[:, :], BB[2][:, 128 * sk:128 * sk + 128], slot[:, :],
                    start=(sk == 0), stop=(sk == nk - 1)), "sMM")
                pv_t[(qt, sk)] = P.val("sMM")
            P.wait("vector", "sMM", pv_t[(qt, nk - 1)])
            P.wait("vector", "sST")
            P.op("vector", lambda e: e.tensor_copy(tb[3][:, :], pC[:, :]),
                 "sVE")
            ev_ve[qt] = P.val("sVE")
            P.wait("scalar", "sMM", pv_t[(qt, nk - 1)])
            P.wait("scalar", "sST")
            P.op("scalar", lambda e: e.activation(
                tf[1][0:1, 0:512], pD[0:1, 0:512], AF.Copy), "sSC")
            ev_sc[qt] = P.val("sSC")
            P.wait("tensor", "sVE", ev_ve[qt])
            P.op("tensor", lambda e, h=h: e.matmul(
                pE[0:16, 0:512], woa_sb[:, 16 * h:16 * h + 16], tb[3][:, :],
                start=True, stop=True), "sMM")
            mp_mm = P.val("sMM")
            P.wait("vector", "sMM", mp_mm)
            P.wait("vector", "sST")
            P.op("vector", lambda e: e.tensor_copy(
                tf[0][0:16, 0:512], pE[0:16, 0:512]), "sVE")
            mp_ev = P.val("sVE")
            P.wait("sync", "sVE", mp_ev)
            P.dma("sync", mp_stage[17 * h:17 * h + 16, 512 * qt:512 * qt + 512],
                  tf[0][0:16, 0:512], "sST")
            P.wait("sync", "sSC", ev_sc[qt])
            P.dma("sync", mp_stage[17 * h + 16:17 * h + 17,
                                   512 * qt:512 * qt + 512],
                  tf[1][0:1, 0:512], "sST")
            P.wait("tensor", "sVE", mp_ev)

    P.wait("gpsimd", "sST")
    P.dma("gpsimd", mp_in.rearrange("(j r) c -> j r c", j=8),
          mp_stage.rearrange("r (j c) -> j r c", j=8), "sGP")
    P.wait("gpsimd", "sGP")
    def _a2a(e):
        with allow_ici_all_to_all():
            return e.collective_compute(
                "AllToAll", OP.bypass, replica_groups=[list(range(8))],
                ins=[mp_in[:, :]], outs=[mp_out[:, :]])
    P.op("gpsimd", _a2a, "sCC")
    g["P1_done"] = True


def build_p2(nc, P, env):
    g = env
    BB, FA, tb, tf, xt = g["BB"], g["FA"], g["tb"], g["tf"], g["xt"]
    small2 = g["small2"]
    ps = g["ps"]
    pA, pB, pC, pD, pE, pF, pT, pS = (ps[k] for k in
        ["pA", "pB", "pC", "pD", "pE", "pF", "pT", "pS"])
    ones_row, ones_col, ones512, ident = (g[k] for k in
        ["ones_row", "ones_col", "ones512", "ident"])
    xn_scr = g["xn_scr"]
    qk6, pw1, pw2, pbr, lw16 = g["qk6"], g["pw1"], g["pw2"], g["pbr"], g["lw16"]
    cw0t, cw1t, cb0t, cb1t = g["cw0t"], g["cw1t"], g["cb0t"], g["cb1t"]
    ind1b, ind2b, bind1, bind2 = g["ind1b"], g["ind2b"], g["bind1"], g["bind2"]
    sl1, sr1, sl2, sr2 = g["sl1"], g["sr1"], g["sl2"], g["sr2"]
    s_in, s_out = g["s_in"], g["s_out"]
    e_scr, den_scr = g["e_scr"], g["den_scr"]
    bind1_sb, bind2_sb = g["bind1_sb"], g["bind2_sb"]
    ind1_sb, ind2_sb = g["ind1_sb"], g["ind2_sb"]
    qkt_sb, spw = g["qkt_sb"], g["spw"]
    spbias, epsr, lwb = g["spbias"], g["epsr"], g["lwb"]
    wtile = g["wtile"]
    AX = mybir.AxisListType.X

    PPL = BB[0]
    XNE = BB[1]
    STR = BB[2]
    SASM = BB[2]
    EB = BB[3]
    SPC = BB[3]
    AVL1, AVL2, RSTc, ATTc = 0, 2048, 4096, 6144

    spoff = {"sl1": 4096, "sr1": 4224, "sl2": 4352, "sr2": 4480}
    P.wait("gpsimd", "sMM")
    P.wait("gpsimd", "sVE")
    P.wait("gpsimd", "sSC")
    for nm, src in (("sl1", sl1), ("sr1", sr1), ("sl2", sl2), ("sr2", sr2)):
        P.dma("gpsimd", SPC[:, spoff[nm]:spoff[nm] + 128], src[:, :], "sGP")
    P.dma("gpsimd", bind1_sb[:, :], bind1[:, :], "sGP")
    P.dma("gpsimd", bind2_sb[:, :], bind2[:, :], "sGP")
    P.dma("gpsimd", ind1_sb[:, :], ind1b[:, :], "sGP")
    P.dma("gpsimd", ind2_sb[:, :], ind2b[:, :], "sGP")
    P.dma("gpsimd", qkt_sb[:, :], qk6[:, :], "sGP")
    P.dma("gpsimd", spw[0:8, 0:16], pw1[:, :], "sGP")
    P.dma("gpsimd", spw[0:8, 16:32], pw2[:, :], "sGP")
    P.dma("gpsimd", spw[0:8, 32:35], cw0t[:, :], "sGP")
    P.dma("gpsimd", spw[0:8, 35:38], cw1t[:, :], "sGP")
    P.dma("gpsimd", spw[0:8, 38:39], cb0t[:, :], "sGP")
    P.dma("gpsimd", spw[0:8, 39:40], cb1t[:, :], "sGP")
    P.dma("gpsimd", spw[0:1, 40:56], pbr[:, :], "sGP")
    P.dma("gpsimd", spw[0:1, 56:72], lw16[:, :], "sGP")
    P.wait("vector", "sSC")
    P.op("vector", lambda e: e.memset(spbias[:, :], -(50.0 + C_SP)), "sVE")
    P.op("vector", lambda e: e.memset(epsr[:, :], EPS_ATT), "sVE")
    P.op("vector", lambda e: e.memset(XNE[:, 2048:2080], 0.0), "sVE")
    gp0 = P.val("sGP")
    P.wait("tensor", "sGP", gp0)
    P.wait("tensor", "sVE")

    P.op("tensor", lambda e: e.matmul(pS[:, 0:16], ones_row[:, :],
                                      spw[0:1, 56:72], start=True, stop=True),
         "sMM")
    P.wait("vector", "sMM")
    P.sop("vector", lambda e: e.tensor_copy(lwb[:, 0:16], pS[:, 0:16]))
    P.wait("tensor", P.selfsem("vector"))
    for br in range(2):
        P.op("tensor", lambda e, br=br: e.matmul(
            pS[0:16, 16:32], qkt_sb[0:8, 16 * br:16 * br + 16],
            qkt_sb[0:8, 32 + 16 * br:32 + 16 * br + 16],
            start=True, stop=True), "sMM")
        P.wait("vector", "sMM")
        P.sop("vector", lambda e, br=br: e.tensor_copy(
            spw[0:16, 80 + 16 * br:80 + 16 * br + 16], pS[0:16, 16:32]))
        P.wait("tensor", P.selfsem("vector"))
    P.sop("vector", lambda e: e.tensor_copy(
        small2[0:8, 20:26], spw[0:8, 32:38]))
    for br in range(2):
        P.sop("vector", lambda e, br=br: e.tensor_copy(
            small2[0:8, 0:16], spw[0:8, 16 * br:16 * br + 16]))
        P.sop("vector", lambda e: e.reduce_sum(
            small2[0:8, 16:17], small2[0:8, 0:16], axis=AX))
        P.sop("vector", lambda e: e.tensor_scalar(
            small2[0:8, 16:17], small2[0:8, 16:17], 1.0 / 16.0, None,
            OP.mult))
        P.sop("vector", lambda e: e.tensor_scalar(
            small2[0:8, 0:16], small2[0:8, 0:16], small2[0:8, 16:17], None,
            OP.subtract))
        P.sop("vector", lambda e, br=br: e.tensor_copy(
            spw[0:8, 112 + 64 * br:112 + 64 * br + 16], small2[0:8, 0:16]))
        for t in range(3):
            P.sop("vector", lambda e, br=br, t=t: e.tensor_scalar(
                spw[0:8, 112 + 64 * br + 16 * (t + 1):
                    112 + 64 * br + 16 * (t + 1) + 16],
                small2[0:8, 0:16], small2[0:8, 20 + 3 * br + t:
                                          21 + 3 * br + t], None, OP.mult))
    P.wait("tensor", P.selfsem("vector"))
    for br in range(2):
        for v in range(4):
            P.op("tensor", lambda e, br=br, v=v: e.matmul(
                pS[0:16, 32 + 16 * v:32 + 16 * v + 16],
                qkt_sb[0:8, 64 + 16 * br:64 + 16 * br + 16],
                spw[0:8, 112 + 64 * br + 16 * v:112 + 64 * br + 16 * v + 16],
                start=True, stop=True), "sMM")
        P.wait("vector", "sMM")
        P.sop("vector", lambda e, br=br: e.tensor_copy(
            spw[0:16, 240 + 64 * br:240 + 64 * br + 64], pS[0:16, 32:96]))
        P.wait("tensor", P.selfsem("vector"))
    for br in range(2):
        P.op("tensor", lambda e, br=br: e.matmul(
            pS[0:1, 96:112], spw[0:8, 38 + br:39 + br],
            spw[0:8, 16 * br:16 * br + 16],
            start=(br == 0), stop=False), None)
    P.op("tensor", lambda e: e.matmul(
        pS[0:1, 96:112], ones_row[0:1, 0:1], spw[0:1, 40:56],
        start=False, stop=True), "sMM")
    P.wait("vector", "sMM")
    P.sop("vector", lambda e: e.tensor_copy(small2[0:1, 0:16],
                                            pS[0:1, 96:112]))
    P.sop("vector", lambda e: e.reduce_sum(
        small2[0:1, 16:17], small2[0:1, 0:16], axis=AX))
    P.sop("vector", lambda e: e.tensor_scalar(
        small2[0:1, 16:17], small2[0:1, 16:17], 1.0 / 16.0, None, OP.mult))
    P.sop("vector", lambda e: e.tensor_scalar(
        small2[0:1, 0:16], small2[0:1, 0:16], small2[0:1, 16:17], None,
        OP.subtract))
    P.sop("vector", lambda e: e.tensor_copy(spw[0:1, 368:384],
                                            small2[0:1, 0:16]))
    P.wait("tensor", P.selfsem("vector"))

    for ti in range(16):
        par = ti % 2
        eb_av = 8192 * par
        eb_re = 8192 * par + 2048
        eb_av2 = 8192 * par + 4096
        eb_re2 = 8192 * par + 6144
        dn1 = 256 * par
        dn2 = 256 * par + 128
        P.wait("sync", P.selfsem("vector"))
        P.wait("sync", "sMM")
        P.dma("sync", XNE[:, 0:2048], xn_scr[128 * ti:128 * ti + 128, :],
              "sLD")
        tld = P.val("sLD")
        P.wait("tensor", "sLD", tld)
        for cq in range(0, 16, 4):
            P.wait("tensor", "sVE")
            for i in range(4):
                c = cq + i
                P.op("tensor", lambda e, c=c, i=i: e.transpose(
                    pT[:, 128 * i:128 * i + 128], XNE[:, c:2048:16],
                    ident[:, :]), "sMM" if i == 3 else None)
            t = P.val("sMM")
            P.wait("vector", "sMM", t)
            P.op("vector", lambda e, cq=cq: e.tensor_copy(
                PPL[:, 128 * cq:128 * cq + 512], pT[:, 0:512]), "sVE")

        for br in range(2):
            m8 = spw[0:16, 80 + 16 * br:80 + 16 * br + 16]
            w4 = spw[0:16, 240 + 64 * br:240 + 64 * br + 64]
            key = "sl1" if br == 0 else "sl2"
            key2 = "sr1" if br == 0 else "sr2"
            slm = SPC[:, spoff[key]:spoff[key] + 128]
            srm = SPC[:, spoff[key2]:spoff[key2] + 128]
            bnd = bind1_sb if br == 0 else bind2_sb
            indt = ind1_sb if br == 0 else ind2_sb
            escr_av = eb_av if br == 0 else eb_av2
            escr_re = eb_re if br == 0 else eb_re2
            dnr = dn1 if br == 0 else dn2
            for grp in range(32):
                P.wait("tensor", "sVE")
                for ss in range(4):
                    if br == 0:
                        in_ap = PPL[:, 4 * grp + ss:2048:128]
                    else:
                        in_ap = XNE[:, 16 * (4 * grp + ss):
                                    16 * (4 * grp + ss) + 16]
                    P.op("tensor", lambda e, in_ap=in_ap, ss=ss: e.transpose(
                        pT[0:16, 128 * ss:128 * ss + 128], in_ap,
                        ident[:, :]), "sMM" if ss == 3 else None)
                t = P.val("sMM")
                P.wait("vector", "sMM", t)
                P.op("vector", lambda e: e.tensor_copy(
                    STR[0:16, 0:512], pT[0:16, 0:512]), "sVE")
                sev = P.val("sVE")
                P.wait("tensor", "sVE", sev)
                for ss in range(4):
                    P.op("tensor", lambda e, ss=ss, w4=w4: e.matmul(
                        pC[:, 64 * ss:64 * ss + 64],
                        STR[0:16, 128 * ss:128 * ss + 128], w4,
                        start=(ss == 0), stop=(ss == 3)), None)
                    P.op("tensor", lambda e, ss=ss, m8=m8: e.matmul(
                        pF[0:16, 128 * ss:128 * ss + 128], m8,
                        STR[0:16, 128 * ss:128 * ss + 128],
                        start=(ss == 0), stop=(ss == 3)),
                        "sMM" if ss == 3 else None)
                vt = P.val("sMM")
                P.wait("vector", "sMM", vt)
                P.op("vector", lambda e: e.tensor_copy(
                    tb[0][:, 0:256], pC[:, 0:256]), "sVE")
                P.op("vector", lambda e: e.tensor_copy(
                    tb[1][0:16, 0:512], pF[0:16, 0:512]), "sVE")
                vev = P.val("sVE")
                P.wait("tensor", "sVE", vev)
                psc = pA if grp % 2 == 0 else pB
                P.wait("tensor", "sSC")
                P.op("tensor", lambda e, psc=psc, bnd=bnd, indt=indt:
                     e.matmul(psc[:, :], bnd[:, :], indt[:, :],
                              start=True, stop=False), None)
                for ss in range(4):
                    P.op("tensor", lambda e, psc=psc, ss=ss: e.matmul(
                        psc[:, 128 * ss:128 * ss + 128],
                        STR[0:16, 128 * ss:128 * ss + 128],
                        tb[1][0:16, 128 * ss:128 * ss + 128],
                        start=False, stop=(ss == 3)),
                        "sMM" if ss == 3 else None)
                sct = P.val("sMM")
                P.wait("scalar", "sMM", sct)
                P.op("scalar", lambda e, psc=psc: e.activation(
                    tb[2][:, :], psc[:, :], AF.Exp, bias=spbias[:, :],
                    scale=float(1.0 / np.sqrt(8.0))), "sSC")
                expt = P.val("sSC")
                P.wait("tensor", "sSC", expt)
                P.wait("tensor", "sVE")
                P.wait("tensor", "sSC")
                for ss in range(4):
                    P.op("tensor", lambda e, ss=ss: e.matmul(
                        pD[0:16, 128 * ss:128 * ss + 128],
                        tb[0][:, 64 * ss:64 * ss + 16],
                        tb[2][:, 128 * ss:128 * ss + 128],
                        start=(ss == 0), stop=(ss == 3)), None)
                for ss in range(4):
                    for v in range(3):
                        mat = (slm, ident, srm)[v]
                        P.op("tensor", lambda e, ss=ss, v=v, mat=mat:
                             e.matmul(
                            pE[0:16, 128 * ss:128 * ss + 128],
                            tb[0][:, 64 * ss + 16 * (v + 1):
                                  64 * ss + 16 * (v + 1) + 16],
                            mat[:, :],
                            start=(ss == 0 and v == 0),
                            stop=(br == 1 and ss == 3 and v == 2)), None)
                    if br == 0:
                        P.op("tensor", lambda e, ss=ss: e.matmul(
                            pE[0:16, 128 * ss:128 * ss + 128],
                            spw[0:1, 368:384], ones512[0:1, 0:128],
                            start=False, stop=(ss == 3)), None)
                P.op("tensor", lambda e: e.matmul(
                    pS[0:1, 0:512], ones_col[:, :], tb[2][:, :],
                    start=True, stop=True), "sMM")
                avt = P.val("sMM")
                P.wait("vector", "sMM", avt)
                P.op("vector", lambda e, grp=grp: e.tensor_copy(
                    EB[0:16, 512 * (grp % 4):512 * (grp % 4) + 512],
                    pD[0:16, 0:512]), "sVE")
                P.op("vector", lambda e, grp=grp: e.tensor_copy(
                    EB[0:16, 2048 + 512 * (grp % 4):
                       2048 + 512 * (grp % 4) + 512],
                    pE[0:16, 0:512]), "sVE")
                ev2 = P.val("sVE")
                P.wait("scalar", "sMM", avt)
                P.op("scalar", lambda e, grp=grp: e.activation(
                    FA[0][0:1, 512 * (grp % 4):512 * (grp % 4) + 512],
                    pS[0:1, 0:512], AF.Copy), "sSC")
                ev3 = P.val("sSC")
                if grp % 4 == 3:
                    gq = grp // 4
                    P.wait("sync", "sVE", ev2)
                    P.dma("sync", e_scr[escr_av + 256 * gq:
                                        escr_av + 256 * gq + 256, :]
                          .rearrange("(gx s c) x -> c gx s x", gx=4, s=4),
                          EB[0:16, 0:2048]
                          .rearrange("c (gx s x) -> c gx s x", gx=4, s=4),
                          "sST")
                    P.dma("sync", e_scr[escr_re + 256 * gq:
                                        escr_re + 256 * gq + 256, :]
                          .rearrange("(gx s c) x -> c gx s x", gx=4, s=4),
                          EB[0:16, 2048:4096]
                          .rearrange("c (gx s x) -> c gx s x", gx=4, s=4),
                          "sST")
                    P.wait("gpsimd", "sSC", ev3)
                    P.dma("gpsimd", den_scr[dnr + 16 * gq:
                                            dnr + 16 * gq + 16, :]
                          .rearrange("(o r) x -> o r x", o=1),
                          FA[0][0:1, 0:2048]
                          .rearrange("o (r x) -> o r x", r=16), "sGP")

        P.wait("sync", "sST")
        P.wait("gpsimd", "sST")
        P.wait("gpsimd", "sVE")
        P.dma("gpsimd", FA[1][:, AVL1:AVL1 + 2048]
              .rearrange("p (c x) -> p c x", c=16),
              e_scr.rearrange("(gr c) x -> gr c x", c=16)
              [eb_av // 16:eb_av // 16 + 128, :, :], "sGP")
        P.dma("gpsimd", FA[1][:, RSTc:RSTc + 2048]
              .rearrange("p (c x) -> p c x", c=16),
              e_scr.rearrange("(gr c) x -> gr c x", c=16)
              [eb_re // 16:eb_re // 16 + 128, :, :], "sGP")
        P.wait("sync", "sVE")
        P.dma("sync", wtile[0][:, 0:2048],
              e_scr[eb_av2:eb_av2 + 2048, :], "sLD", transpose=True)
        P.dma("sync", wtile[2][:, 0:2048],
              e_scr[eb_re2:eb_re2 + 2048, :], "sLD", transpose=True)
        P.dma("sync", tb[3][:, 0:128], den_scr[dn1:dn1 + 128, :], "sLD")
        P.dma("sync", tb[3][:, 128:256], den_scr[dn2:dn2 + 128, :], "sLD",
              transpose=True)
        mld = P.val("sLD")
        mgp = P.val("sGP")
        P.wait("scalar", "sLD", mld)
        P.sop("scalar", lambda e: e.activation(
            tf[1][:, 0:256], tb[3][:, 0:256], AF.Ln))
        P.sop("scalar", lambda e: e.activation(
            tf[1][:, 0:256], tf[1][:, 0:256], AF.Exp, scale=-1.0))
        rd_done = P.val(P.selfsem("scalar"))
        P.wait("vector", P.selfsem("scalar"), rd_done)
        P.wait("vector", "sLD", mld)
        P.wait("vector", "sGP", mgp)
        P.sop("vector", lambda e: e.tensor_copy(
            FA[1][:, AVL2:AVL2 + 2048].rearrange("p (c y) -> p c y", c=16),
            wtile[0][:, 0:2048].rearrange("p (q pp c) -> p c (q pp)",
                                          q=32, pp=4)))
        P.sop("vector", lambda e: e.tensor_tensor(
            FA[1][:, RSTc:RSTc + 2048].rearrange("p (c y) -> p c y", c=16),
            FA[1][:, RSTc:RSTc + 2048].rearrange("p (c y) -> p c y", c=16),
            wtile[2][:, 0:2048].rearrange("p (q pp c) -> p c (q pp)",
                                          q=32, pp=4), OP.add))
        for c in range(16):
            av1 = FA[1][:, AVL1 + 128 * c:AVL1 + 128 * c + 128]
            av2 = FA[1][:, AVL2 + 128 * c:AVL2 + 128 * c + 128]
            rst = FA[1][:, RSTc + 128 * c:RSTc + 128 * c + 128]
            att = FA[1][:, ATTc + 128 * c:ATTc + 128 * c + 128]
            P.sop("vector", lambda e, av1=av1, att=att: e.tensor_tensor(
                att, av1, tf[1][:, 0:128], OP.mult))
            P.sop("vector", lambda e, av2=av2: e.tensor_tensor(
                xt[1][:, 0:128], av2, tf[1][:, 128:256], OP.mult))
            P.sop("vector", lambda e, att=att: e.tensor_tensor(
                att, att, xt[1][:, 0:128], OP.add))
            P.sop("vector", lambda e, att=att, rst=rst: e.tensor_tensor(
                att, att, rst, OP.add))
        att_done = P.val(P.selfsem("vector"))
        P.wait("scalar", P.selfsem("vector"), att_done)
        P.sop("scalar", lambda e: e.activation(
            FA[0][:, 0:2048], FA[1][:, ATTc:ATTc + 2048], AF.Square))
        P.wait("vector", P.selfsem("scalar"))
        P.sop("vector", lambda e: e.reduce_sum(
            xt[1][:, 0:128],
            FA[0][:, 0:2048].rearrange("p (c y) -> p y c", c=16),
            axis=AX))
        P.wait("scalar", P.selfsem("vector"))
        P.sop("scalar", lambda e: e.activation(
            xt[1][:, 128:256], xt[1][:, 0:128], AF.Ln,
            scale=float(1.0 / 16.0), bias=epsr[:, :]))
        P.sop("scalar", lambda e: e.activation(
            xt[1][:, 128:256], xt[1][:, 128:256], AF.Exp, scale=-0.5))
        rstd_done = P.val(P.selfsem("scalar"))
        P.wait("vector", P.selfsem("scalar"), rstd_done)
        for c in range(16):
            att = FA[1][:, ATTc + 128 * c:ATTc + 128 * c + 128]
            P.sop("vector", lambda e, c=c: e.tensor_scalar(
                xt[1][:, 256:384], xt[1][:, 128:256], lwb[:, c:c + 1], None,
                OP.mult))
            P.sop("vector", lambda e, att=att: e.tensor_tensor(
                xt[1][:, 384:512], att, xt[1][:, 256:384], OP.mult))
            P.sop("vector", lambda e, c=c: e.tensor_tensor(
                SASM[:, 1024 + 128 * c:1024 + 128 * c + 128],
                xt[1][:, 384:512], XNE[:, c:2048:16], OP.add))
        sdone = P.val(P.selfsem("vector"))
        P.wait("sync", P.selfsem("vector"), sdone)
        for k in range(16):
            dst_row = 256 * (k // 2) + 128 * (k % 2) + 8 * ti
            P.dma("sync",
                  s_in[dst_row:dst_row + 8, :]
                  .rearrange("r (o x) -> (r o) x", o=16),
                  SASM[:, 1024 + 128 * k:1024 + 128 * k + 128], "sST")

    P.wait("gpsimd", "sST")

    def _a2a_s(e):
        with allow_ici_all_to_all():
            return e.collective_compute(
                "AllToAll", OP.bypass, replica_groups=[list(range(8))],
                ins=[s_in[:, :]], outs=[s_out[:, :]])
    P.op("gpsimd", _a2a_s, "sCC")
    g["P2_done"] = True


def build_p3(nc, P, env):
    g = env
    stage = g["stage"]
    BB, FA, tb, tf, xt = g["BB"], g["FA"], g["tb"], g["tf"], g["xt"]
    wtile, small, small2 = g["wtile"], g["small"], g["small2"]
    ps = g["ps"]
    pA, pB, pC, pD, pE, pF, pT, pS = (ps[k] for k in
        ["pA", "pB", "pC", "pD", "pE", "pF", "pT", "pS"])
    ones_row, ident, lnb = g["ones_row"], g["ident"], g["lnb"]
    identf = g["identf"]
    a_sb, bmat_sb, modac, m2t = g["a_sb"], g["bmat_sb"], g["modac"], g["m2t"]
    xm, y = g["xm"], g["y"]
    ln2row = g["ln2row"]
    wi_t, wu_t, wd_t, bmat = g["wi_t"], g["wu_t"], g["wd_t"], g["bmat"]
    mp_out, s_out = g["mp_out"], g["s_out"]
    xnm_scr, h_scr, hT_scr, mlp_scr = (g[k] for k in
        ["xnm_scr", "h_scr", "hT_scr", "mlp_scr"])
    lbk = g["lbk"]
    p0_tile = g["p0_tile"]
    AX = mybir.AxisListType.X
    SV, SS = P.selfsem("vector"), P.selfsem("scalar")

    P.wait("tensor", "sSC")
    P.wait("tensor", "sVE")
    P.wait("tensor", "sMM")
    for i in range(4):
        P.op("tensor", lambda e, i=i: e.matmul(
            pA[:, 0:512], ones_row[:, :], ln2row[:, 512 * i:512 * (i + 1)],
            start=True, stop=True), "sMM")
        P.wait("vector", "sMM")
        P.op("vector", lambda e, i=i: e.tensor_copy(
            lnb[:, 512 * i:512 * (i + 1)], pA[:, 0:512]), "sVE")
        P.wait("tensor", "sVE")
    if stage >= 4:
        P.dma("gpsimd", tb[0][0:1, 0:2], lbk[:, :], "sGP")
        P.wait("tensor", "sGP")
        for kp in range(2):
            P.op("tensor", lambda e, kp=kp: e.matmul(
                pA[:, 0:1], ones_row[:, :], tb[0][0:1, kp:kp + 1],
                start=True, stop=True), "sMM")
            P.wait("vector", "sMM")
            P.sop("vector", lambda e, kp=kp: e.tensor_copy(
                small2[:, 22 + kp:23 + kp], pA[:, 0:1]))
            P.wait("tensor", P.selfsem("vector"))

    for i in range(16):
        p0_tile(i, xm, lnb, xnm_scr, h_scr)

    P.wait("sync", "sST")
    for dsi in range(16):
        buf = wtile[dsi % 2]
        P.wait("sync", "sST")
        P.dma("sync", buf[:, 0:2048],
              h_scr[0:2048, 128 * dsi:128 * (dsi + 1)], "sLD",
              transpose=True)
        P.wait("sync", "sLD")
        P.dma("sync", hT_scr[128 * dsi:128 * (dsi + 1), :], buf[:, 0:2048],
              "sST")

    P.wait("sync", "sCC")
    P.wait("vector", "sSC")
    P.op("vector", lambda e: e.memset(modac[:, 0:256], 0.0), "sVE")
    for i in range(8):
        for h in range(4):
            P.wait("sync", P.selfsem("vector"))
            P.dma("sync", FA[1][0:17, 0:1024],
                  mp_out[68 * i + 17 * h:68 * i + 17 * h + 17, :], "sLD")
            mld = P.val("sLD")
            P.wait("tensor", "sLD", mld)
            for u in range(8):
                P.wait("tensor", P.selfsem("vector"))
                P.op("tensor", lambda e, u=u: e.transpose(
                    pS[:, 0:17], FA[1][0:17, 128 * u:128 * u + 128],
                    identf[0:17, 0:17]), "sMM")
                P.wait("scalar", "sMM")
                P.sop("scalar", lambda e: e.activation(
                    small2[:, 20:21], pS[:, 16:17], AF.Ln))
                P.sop("scalar", lambda e: e.activation(
                    small2[:, 21:22], small2[:, 20:21], AF.Exp, scale=-1.0))
                ch = 8 * (i // 4) + u
                P.wait("vector", P.selfsem("scalar"))
                P.sop("vector", lambda e: e.tensor_scalar(
                    small2[:, 0:16], pS[:, 0:16], small2[:, 21:22], None,
                    OP.mult))
                P.sop("vector", lambda e, ch=ch: e.tensor_tensor(
                    modac[:, 16 * ch:16 * ch + 16],
                    modac[:, 16 * ch:16 * ch + 16],
                    small2[:, 0:16], OP.add))
    P.wait("scalar", P.selfsem("vector"))
    P.sop("scalar", lambda e: e.activation(
        modac[:, 0:256], modac[:, 0:256], AF.Tanh))
    mod_done = P.val(P.selfsem("scalar"))

    for hf in range(2):
        P.wait("sync", "sMM")
        P.wait("sync", "sST")
        for dsi in range(16):
            P.dma("sync",
                  BB[dsi // 8][:, 1024 * (dsi % 8):1024 * (dsi % 8) + 1024],
                  hT_scr[128 * dsi:128 * dsi + 128,
                         1024 * hf:1024 * hf + 1024], "sLD")
        hld = P.val("sLD")

        P.wait("tensor", "sLD", hld)
        P.wait("tensor", P.selfsem("scalar"), mod_done)
        for ch in range(8):
            bank = pE if ch % 2 == 0 else pF
            if ch >= 2:
                P.wait("tensor", P.selfsem("vector"))
            for ds in range(16):
                P.op("tensor", lambda e, ch=ch, ds=ds, bank=bank: e.matmul(
                    bank[:, 0:16],
                    BB[ds // 8][:, 1024 * (ds % 8) + 128 * ch:
                                1024 * (ds % 8) + 128 * ch + 128],
                    a_sb[:, 16 * ds:16 * ds + 16],
                    start=(ds == 0), stop=(ds == 15)), "sMM")
            P.wait("scalar", "sMM")
            P.sop("scalar", lambda e, bank=bank: e.activation(
                small2[:, 32:48], bank[:, 0:16], AF.Identity, scale=0.1))
            gch = 8 * hf + ch
            P.wait("vector", P.selfsem("scalar"))
            P.sop("vector", lambda e, gch=gch: e.tensor_tensor(
                tb[0][:, 0:16], modac[:, 16 * gch:16 * gch + 16],
                small2[:, 32:48], OP.mult))
            P.wait("tensor", P.selfsem("vector"))
            P.op("tensor", lambda e: e.transpose(
                pT[0:16, 0:128], tb[0][:, 0:16], ident[:, :]), "sMM")
            P.wait("vector", "sMM")
            P.sop("vector", lambda e, ch=ch, hf=hf: e.tensor_copy(
                m2t[0:16, 1024 * hf + 128 * ch:1024 * hf + 128 * ch + 128],
                pT[0:16, 0:128]))
        m2_done = P.val(P.selfsem("vector"))
        P.wait("tensor", P.selfsem("vector"), m2_done)

        gu_t = {}
        P.wait("gpsimd", "sMM")
        for su in range(4):
            P.wait("gpsimd", "sMM")
            P.dma("gpsimd", bmat_sb[:, :],
                  bmat[:, 2048 * su:2048 * su + 2048], "sGP")
            bm_ld = P.val("sGP")
            P.wait("tensor", "sGP", bm_ld)
            for fc0 in range(16):
                fc = 16 * su + fc0
                wslot_i = wtile[0] if fc0 % 2 == 0 else wtile[2]
                wslot_u = wtile[1] if fc0 % 2 == 0 else wtile[3]
                if fc0 >= 2:
                    P.wait("gpsimd", "sMM", gu_t[(su, fc0 - 2, 1)])
                elif su > 0:
                    P.wait("gpsimd", "sMM", gu_t[(su - 1, 15, 1)])
                P.dma("gpsimd", wslot_i[:, :],
                      wi_t[:, 2048 * fc:2048 * fc + 2048], "sGP")
                P.dma("gpsimd", wslot_u[:, :],
                      wu_t[:, 2048 * fc:2048 * fc + 2048], "sGP")
                wld = P.val("sGP")
                P.wait("tensor", "sGP", wld)
                for tt in range(2):
                    pg = pA if tt == 0 else pC
                    pu = pB if tt == 0 else pD
                    P.wait("tensor", "sSC")
                    P.wait("tensor", "sVE")
                    for ds in range(16):
                        rhs = BB[ds // 8][:, 1024 * (ds % 8) + 512 * tt:
                                          1024 * (ds % 8) + 512 * tt + 512]
                        P.op("tensor", lambda e, ds=ds, rhs=rhs, pg=pg,
                             ws=wslot_i: e.matmul(
                            pg[:, :], ws[:, 128 * ds:128 * ds + 128], rhs,
                            start=(ds == 0), stop=False), None)
                        P.op("tensor", lambda e, ds=ds, rhs=rhs, pu=pu,
                             ws=wslot_u: e.matmul(
                            pu[:, :], ws[:, 128 * ds:128 * ds + 128], rhs,
                            start=(ds == 0), stop=(ds == 15)), None)
                    P.op("tensor", lambda e, fc=fc, tt=tt, hf=hf, pg=pg:
                         e.matmul(
                        pg[:, :], bmat_sb[:, 128 * fc0:128 * fc0 + 128],
                        m2t[:, 1024 * hf + 512 * tt:
                            1024 * hf + 512 * tt + 512],
                        start=False, stop=True), "sMM")
                    mm_gu = P.val("sMM")
                    gu_t[(su, fc0, tt)] = mm_gu
                    gusb = BB[2] if tt == 0 else BB[3]
                    P.wait("scalar", "sMM", mm_gu)
                    P.op("scalar", lambda e, pg=pg: e.activation(
                        tb[2][:, :], pg[:, :], AF.Silu), "sSC")
                    P.wait("vector", "sMM", mm_gu)
                    P.op("vector", lambda e, pu=pu: e.tensor_copy(
                        tb[3][:, :], pu[:, :]), "sVE")
                    P.wait("vector", "sSC")
                    P.op("vector", lambda e, gusb=gusb, fc0=fc0:
                         e.tensor_tensor(
                        gusb[:, 512 * fc0:512 * fc0 + 512], tb[2][:, :],
                        tb[3][:, :], OP.mult), "sVE")
            gu_done = P.val("sVE")
            for tt in range(2):
                gusb = BB[2] if tt == 0 else BB[3]
                acc = FA[0] if tt == 0 else FA[1]
                for dc in range(16):
                    wslot = wtile[0] if dc % 2 == 0 else wtile[1]
                    P.wait("gpsimd", "sMM")
                    P.dma("gpsimd",
                          wslot[:, :].rearrange("p (f j) -> p f j", f=16),
                          wd_t.rearrange("p (f d j) -> p f d j", f=64, d=16)
                          [:, 16 * su:16 * su + 16, dc, :], "sGP")
                    wld2 = P.val("sGP")
                    P.wait("tensor", "sGP", wld2)
                    P.wait("tensor", "sVE", gu_done)
                    pdn = pA if dc % 2 == 0 else pB
                    P.wait("tensor", "sVE")
                    for fc0 in range(16):
                        P.op("tensor", lambda e, fc0=fc0, wslot=wslot,
                             gusb=gusb, pdn=pdn: e.matmul(
                            pdn[:, :], wslot[:, 128 * fc0:128 * fc0 + 128],
                            gusb[:, 512 * fc0:512 * fc0 + 512],
                            start=(fc0 == 0), stop=(fc0 == 15)),
                            "sMM" if fc0 == 15 else None)
                    dmm = P.val("sMM")
                    P.wait("vector", "sMM", dmm)
                    if su == 0:
                        P.op("vector", lambda e, acc=acc, dc=dc, pdn=pdn:
                             e.tensor_copy(
                            acc[:, 512 * dc:512 * dc + 512], pdn[:, :]),
                            "sVE")
                    else:
                        P.op("vector", lambda e, acc=acc, dc=dc, pdn=pdn:
                             e.tensor_tensor(
                            acc[:, 512 * dc:512 * dc + 512],
                            acc[:, 512 * dc:512 * dc + 512], pdn[:, :],
                            OP.add), "sVE")

        P.wait("vector", "sMM")
        for tt in range(2):
            acc = FA[0] if tt == 0 else FA[1]
            P.wait("vector", "sST")
            P.op("vector", lambda e, acc=acc: e.tensor_copy(
                BB[2][:, 0:8192], acc[:, 0:8192]), "sVE")
            cast_t = P.val("sVE")
            P.wait("sync", "sVE", cast_t)
            P.dma("sync",
                  mlp_scr.rearrange("(d p) t -> p d t", p=128)
                  [:, :, 512 * tt:512 * tt + 512],
                  BB[2][:, 0:8192].rearrange("p (d t) -> p d t", d=16),
                  "sST")
        if hf == 0 and "mlps0" in g["dbg"]:
            g["mkdbg"]("mlps0", (2048, 1024), BF16)
            P.wait("sync", "sST")
            g["dump_scr"]("mlps0", mlp_scr, 2048, 1024)
        for ti in range(8):
            row0 = 1024 * hf + 128 * ti
            P.wait("sync", "sST")
            for dg in range(4):
                P.dma("sync", BB[3][:, 512 * dg:512 * dg + 512],
                      mlp_scr[512 * dg:512 * dg + 512,
                              128 * ti:128 * ti + 128],
                      "sLD", transpose=True)
            P.dma("sync", wtile[2][:, 0:2048],
                  xnm_scr[row0:row0 + 128, :], "sLD")
            P.dma("sync", xt[0][:, 0:2048], xm[row0:row0 + 128, :], "sLD")
            if stage >= 4:
                i_s = 4 * hf + (ti % 4)
                kp = ti // 4
                P.dma("sync", wtile[3][:, 0:2048],
                      s_out[256 * i_s + 128 * kp:256 * i_s + 128 * kp + 128,
                            :], "sLD")
            ld2 = P.val("sLD")
            P.wait("vector", "sLD", ld2)
            P.wait("vector", "sST")
            P.sop("vector", lambda e: e.tensor_tensor(
                xt[1][:, 0:2048], wtile[2][:, 0:2048], BB[3][:, 0:2048],
                OP.add))
            P.wait("scalar", P.selfsem("vector"))
            P.sop("scalar", lambda e: e.activation(
                wtile[1][:, 0:2048], xt[1][:, 0:2048], AF.Silu))
            P.wait("vector", P.selfsem("scalar"))
            if stage >= 4:
                kp = ti // 4
                P.sop("vector", lambda e, kp=kp: e.tensor_scalar(
                    FA[0][:, 0:2048], wtile[3][:, 0:2048],
                    small2[:, 22 + kp:23 + kp], None, OP.add))
                P.sop("vector", lambda e: e.tensor_tensor(
                    FA[0][:, 0:2048], FA[0][:, 0:2048], wtile[1][:, 0:2048],
                    OP.mult))
                P.sop("vector", lambda e: e.tensor_tensor(
                    xt[1][:, 0:2048], FA[0][:, 0:2048], xt[0][:, 0:2048],
                    OP.add))
            else:
                P.sop("vector", lambda e: e.tensor_copy(
                    xt[1][:, 0:2048], wtile[1][:, 0:2048]))
            P.wait("sync", P.selfsem("vector"))
            P.dma("sync", y[row0:row0 + 128, :], xt[1][:, 0:2048], "sST")
    if "hta" in g["dbg"]:
        P.wait("sync", "sST")
        g["dump_scr"]("hta", hT_scr, 2048, 2048)
    if "mlps" in g["dbg"]:
        P.wait("sync", "sST")
        g["dump_scr"]("mlps", mlp_scr, 2048, 1024)
    if "modd" in g["dbg"]:
        g["mkdbg"]("modd", (128, 256), F32)
        P.wait("sync", P.selfsem("scalar"))
        P.wait("sync", P.selfsem("vector"))
        P.dma("sync", g["dbg_out"]["modd"][:, :], modac[:, 0:256], "sDB")
    if "m2ts" in g["dbg"]:
        g["mkdbg"]("m2ts", (16, 2048), BF16)
        P.wait("sync", P.selfsem("vector"))
        P.dma("sync", g["dbg_out"]["m2ts"][:, :], m2t[:, :], "sDB")
    g["P3_done"] = True



def make_consts():
    ident = np.eye(128, dtype=BF)
    cm = np.zeros((4, 128, 512), np.float32)
    for m in range(4):
        for k in range(128):
            for q0 in range(512):
                if 128 * m + k > q0:
                    cm[m, k, q0] = -1e30
    ind1 = np.zeros((16, 128), np.float32)
    for w in range(16):
        ind1[w, 8 * w:8 * w + 8] = 1
    ind2 = np.zeros((16, 128), np.float32)
    for m in range(4):
        for j in range(4):
            ind2[m * 4 + j, 32 * m + 4 * np.arange(8) + j] = 1
    SL1 = np.zeros((128, 128), np.float32); SR1 = np.zeros((128, 128), np.float32)
    for t in range(128):
        if t % 8 != 0: SL1[t - 1, t] = 1
        if t % 8 != 7: SR1[t + 1, t] = 1
    SL2 = np.zeros((128, 128), np.float32); SR2 = np.zeros((128, 128), np.float32)
    for r in range(128):
        tp = (r % 32) // 4
        if tp != 0: SL2[r - 4, r] = 1
        if tp != 7: SR2[r + 4, r] = 1
    return dict(
        identw=ident,
        cmask=cm.reshape(512, 512),
        ind1b=np.tile(ind1, (1, 4)).astype(BF),
        ind2b=np.tile(ind2, (1, 4)).astype(BF),
        bind1=(BIG_SP * ind1).astype(BF),
        bind2=(BIG_SP * ind2).astype(BF),
        sl1=SL1.astype(BF), sr1=SR1.astype(BF),
        sl2=SL2.astype(BF), sr2=SR2.astype(BF),
    )


def make_inputs(inp):
    x = np.ascontiguousarray(inp["L0_attn_x"], np.float32)
    Wq = inp["L0_ffn_Wq"]; Wk = inp["L0_ffn_Wk"]; Wv = inp["L0_ffn_Wv"]
    Wo = inp["L0_ffn_Wo"]; A = inp["L0_ffn_A"]; Bm = inp["L0_ffn_B"]
    Wi = inp["L0_ffn_W_in_base"]; Wu = inp["L0_ffn_W_up_base"]
    Wd = inp["L0_ffn_W_down"]
    consts = make_consts()

    def tile_lhsT(Wslice):
        arr = np.asarray(Wslice, np.float32).reshape(4, 128, 16, 128)
        return np.ascontiguousarray(arr.transpose(3, 2, 0, 1).reshape(128, 8192))

    wi_t = np.ascontiguousarray(
        np.asarray(Wi, np.float32).reshape(64, 128, 16, 128)
        .transpose(3, 0, 2, 1).reshape(128, 131072))
    wu_t = np.ascontiguousarray(
        np.asarray(Wu, np.float32).reshape(64, 128, 16, 128)
        .transpose(3, 0, 2, 1).reshape(128, 131072))
    wd_t = np.ascontiguousarray(
        np.asarray(Wd, np.float32).reshape(16, 128, 64, 128)
        .transpose(3, 2, 0, 1).reshape(128, 131072))
    a_tt = np.ascontiguousarray(
        np.asarray(A, np.float32).reshape(16, 128, 16)
        .transpose(1, 0, 2).reshape(128, 256))

    ins = []
    for c in range(8):
        Bc, ac = c // 4, c % 4
        sl = slice(512 * ac, 512 * (ac + 1))
        wo_sl = np.asarray(Wo, np.float32)[:, sl]
        wo_s = np.ascontiguousarray(
            wo_sl.reshape(16, 128, 4, 128).transpose(1, 0, 2, 3)
            .reshape(128, 8192))
        d = dict(
            xq=np.ascontiguousarray(x[Bc, 2048 * ac:2048 * (ac + 1), :]),
            xm=np.ascontiguousarray(
                x[:, 1024 * c:1024 * (c + 1), :].reshape(2048, 2048)),
            wq_t=tile_lhsT(np.asarray(Wq, np.float32)[sl, :]),
            wk_t=tile_lhsT(np.asarray(Wk, np.float32)[sl, :]),
            wv_t=tile_lhsT(np.asarray(Wv, np.float32)[sl, :]),
            wo_s=wo_s,
            a_t=a_tt,
            wi_t=wi_t, wu_t=wu_t, wd_t=wd_t,
            bmat=np.ascontiguousarray(np.asarray(Bm, np.float32)),
            ln1r=np.asarray(inp["L0_ffn_ln1_w"], np.float32).reshape(1, 2048),
            ln2r=np.asarray(inp["L0_ffn_ln2_w"], np.float32).reshape(1, 2048),
            qkvw=np.ascontiguousarray(np.asarray(inp["L0_attn_qkv_w"], np.float32)),
            cw0t=np.ascontiguousarray(np.asarray(inp["L0_attn_conv_w0"], np.float32)[:, 0, 1, :]),
            cw1t=np.ascontiguousarray(np.asarray(inp["L0_attn_conv_w1"], np.float32)[:, 0, :, 1]),
            cb0=np.asarray(inp["L0_attn_conv_b0"], np.float32).reshape(1, 8),
            cb1=np.asarray(inp["L0_attn_conv_b1"], np.float32).reshape(1, 8),
            qk6=np.ascontiguousarray(np.hstack([
                np.asarray(inp["L0_attn_qkv_w"], np.float32)[:, 8 * j:8 * j + 8].T
                for j in range(6)])),
            pw1=np.ascontiguousarray(np.asarray(inp["L0_attn_proj_w"], np.float32)[0:8, :]),
            pw2=np.ascontiguousarray(np.asarray(inp["L0_attn_proj_w"], np.float32)[8:16, :]),
            cb0t=np.asarray(inp["L0_attn_conv_b0"], np.float32).reshape(8, 1),
            cb1t=np.asarray(inp["L0_attn_conv_b1"], np.float32).reshape(8, 1),
            pwm=np.ascontiguousarray(np.asarray(inp["L0_attn_proj_w"], np.float32)),
            pbr=np.asarray(inp["L0_attn_proj_b"], np.float32).reshape(1, 16),
            lw16=np.asarray(inp["L0_attn_ln_w"], np.float32).reshape(1, 16),
            lbk=np.asarray(inp["L0_attn_ln_b"], np.float32)[2 * c:2 * c + 2].reshape(1, 2),
            **consts,
        )
        ins.append(d)
    return ins


def kernel(**inputs):
    ins = make_inputs(inputs)
    nc, _ = build(stage=4)
    res = run_bass_kernel_spmd(nc, ins, core_ids=list(range(8)))
    out = np.zeros((2, 8192, 2048), np.float32)
    for c in range(8):
        yc = res.results[c]["y"].reshape(2, 1024, 2048)
        out[:, 1024 * c:1024 * (c + 1), :] = yc
    return out


